# version 03 (final)
# baseline (speedup 1.0000x reference)
import math
from contextlib import ExitStack

import numpy as np
import ml_dtypes

import concourse.bass as bass
import concourse.mybir as mybir
from concourse.bass_utils import run_bass_kernel_spmd

F32 = mybir.dt.float32
BF16 = mybir.dt.bfloat16
I32 = mybir.dt.int32
AF = mybir.ActivationFunctionType
ALU = mybir.AluOpType
AX = mybir.AxisListType
BF = ml_dtypes.bfloat16

S = 8192
FD = 4096
NH = 32
TOK = 1024
NT = 8
IFD = 11008
NIF = 86
OFD = 4096
THRESH = 160.0
MARGIN = 2.0
SW = 1024
NSW = 8
DMA = 16

SLOPES = [2.0 ** (-(h + 1) / 4.0) for h in range(NH)]
EH = [max(1536, int(min(S, 512 * math.ceil((THRESH / sl + TOK) / 512.0))))
      for sl in SLOPES]
GH = []
for sl in SLOPES:
    g = 512
    while g > 64 and sl * g > 55.0:
        g //= 2
    GH.append(g)


def win_range(h, lt):
    E = EH[h]
    hi = E - TOK + 128 * (lt + 1)
    lo = max(0, int(hi - 128 - THRESH / SLOPES[h]) // 128 * 128)
    return lo, hi


class Prog:
    def __init__(self):
        self.ops = []
        self.n = 0

    def add(self, engine, fn, dma=False):
        self.ops.append((engine, fn, self.n, (DMA if dma else 1)))
        self.n += DMA if dma else 1

    def emit(self, block, seq):
        engines = ["sync", "gpsimd", "scalar", "vector", "tensor"]
        for g in engines:
            lst = [op for op in self.ops if op[0] == g]
            if not lst:
                continue

            def body(eng, lst=lst):
                for (_, fn, wait_val, inc) in lst:
                    if wait_val > 0:
                        eng.wait_ge(seq, wait_val)
                    ins = fn(eng)
                    assert ins is not None
                    ins.then_inc(seq, inc)

            getattr(block, g)(body)
        total = self.n

        def fin(eng):
            eng.wait_ge(seq, total)

        block.sync(fin)


def sb(t, p0, np_, c0, nc_):
    a = t.ap()
    return bass.AP(a.tensor, a.offset + p0 * a.ap[0][0] + c0,
                   [[a.ap[0][0], np_], [1, nc_]])


def mkap(ap, offset, dims):
    return bass.AP(ap.tensor, ap.offset + offset, [list(d) for d in dims])


def build_nc():
    nc = bass.Bass("TRN2", target_bir_lowering=False)

    xk = nc.dram_tensor("xk", [S, FD], F32, kind="ExternalInput").ap()
    w_attn = nc.dram_tensor("w_attn", [FD, 3 * FD], F32, kind="ExternalInput").ap()
    b_attn = nc.dram_tensor("b_attn", [3 * FD], F32, kind="ExternalInput").ap()
    w_proj = nc.dram_tensor("w_proj", [FD, FD], F32, kind="ExternalInput").ap()
    b_proj = nc.dram_tensor("b_proj", [FD], F32, kind="ExternalInput").ap()
    gqw = nc.dram_tensor("gqw", [FD, IFD // 8], I32, kind="ExternalInput").ap()
    gsc = nc.dram_tensor("gsc", [32, IFD], F32, kind="ExternalInput").ap()
    gzr = nc.dram_tensor("gzr", [32, IFD // 8], I32, kind="ExternalInput").ap()
    uqw = nc.dram_tensor("uqw", [FD, IFD // 8], I32, kind="ExternalInput").ap()
    usc = nc.dram_tensor("usc", [32, IFD], F32, kind="ExternalInput").ap()
    uzr = nc.dram_tensor("uzr", [32, IFD // 8], I32, kind="ExternalInput").ap()
    dqw = nc.dram_tensor("dqw", [IFD, OFD // 8], I32, kind="ExternalInput").ap()
    dsc = nc.dram_tensor("dsc", [NIF, OFD], F32, kind="ExternalInput").ap()
    dzr = nc.dram_tensor("dzr", [NIF, OFD // 8], I32, kind="ExternalInput").ap()
    phant = nc.dram_tensor("phant", [128, NH * 64], F32, kind="ExternalInput").ap()
    out = nc.dram_tensor("out", [TOK, FD], F32, kind="ExternalOutput").ap()

    ident_d = nc.inline_tensor(np.eye(128, dtype=BF), "ident_d").ap()
    iota_d = nc.inline_tensor(np.arange(128, dtype=np.float32).reshape(128, 1),
                              "iota_d").ap()
    mt_np = np.full((128, 384), -1e30, dtype=np.float32)
    for i in range(128):
        mt_np[i, i + 128:] = 0.0
    mt_d = nc.inline_tensor(mt_np, "mt_d").ap()
    onec_d = nc.inline_tensor(np.ones((128, 1), dtype=BF), "onec_d").ap()
    oner_d = nc.inline_tensor(np.ones((1, 128), dtype=BF), "oner_d").ap()

    w16kv = nc.dram_tensor("w16kv", [FD, 2 * FD], BF16, kind="Internal").ap()
    kst = [nc.dram_tensor(f"kst{h}", [128, EH[h]], BF16, kind="Internal").ap()
           for h in range(NH)]
    vst = [nc.dram_tensor(f"vst{h}", [128, EH[h]], BF16, kind="Internal").ap()
           for h in range(NH)]
    qst = nc.dram_tensor("qst", [FD, TOK], BF16, kind="Internal").ap()
    hst = nc.dram_tensor("hst", [IFD, TOK], BF16, kind="Internal").ap()
    atst = nc.dram_tensor("atst", [FD, TOK], BF16, kind="Internal").ap()
    zs_g = nc.dram_tensor("zs_g", [32, IFD], BF16, kind="Internal").ap()
    zs_u = nc.dram_tensor("zs_u", [32, IFD], BF16, kind="Internal").ap()
    zs_d = nc.dram_tensor("zs_d", [NIF, OFD], BF16, kind="Internal").ap()
    gs16 = nc.dram_tensor("gs16", [32, IFD], BF16, kind="Internal").ap()
    us16 = nc.dram_tensor("us16", [32, IFD], BF16, kind="Internal").ap()
    ds16 = nc.dram_tensor("ds16", [NIF, OFD], BF16, kind="Internal").ap()
    mrow = nc.dram_tensor("mrow", [1, NH], F32, kind="Internal").ap()

    P = Prog()

    with ExitStack() as ctx:
        e = ctx.enter_context
        big = e(nc.sbuf_tensor("big", [128, 32 * SW], BF16))
        xtile = e(nc.sbuf_tensor("xtile", [128, FD], BF16))
        wsl = e(nc.sbuf_tensor("wsl", [128, 4096], BF16))
        stg = [e(nc.sbuf_tensor(f"stg{i}", [128, 512], BF16)) for i in range(4)]
        stats = e(nc.sbuf_tensor("stats", [128, 32], F32))
        rowf = e(nc.sbuf_tensor("rowf", [1, 512], BF16))
        bsb = e(nc.sbuf_tensor("bsb", [128, 96], F32))
        bq = e(nc.sbuf_tensor("bq", [128, 32], F32))
        xsum = e(nc.sbuf_tensor("xsum", [128, NT * 32], F32))
        xsum16 = e(nc.sbuf_tensor("xsum16", [128, 32], BF16))
        xsumT = e(nc.sbuf_tensor("xsumT", [32, TOK], BF16))
        phsh = e(nc.sbuf_tensor("phsh", [128, 64], F32))
        m_rep = e(nc.sbuf_tensor("m_rep", [128, 1], F32))
        base_h = e(nc.sbuf_tensor("base_h", [128, 1], F32))
        biasv = e(nc.sbuf_tensor("biasv", [128, 32], F32))
        nrm = e(nc.sbuf_tensor("nrm", [1, 8], F32))
        recipv = e(nc.sbuf_tensor("recipv", [128, 8], F32))
        ktile = e(nc.sbuf_tensor("ktile", [128, 128], BF16))
        vtile = e(nc.sbuf_tensor("vtile", [128, 128], BF16))
        qbuf = e(nc.sbuf_tensor("qbuf", [128, TOK], BF16))
        ptile = [e(nc.sbuf_tensor(f"ptile{i}", [128, 512], BF16)) for i in range(2)]
        pkg = [e(nc.sbuf_tensor(f"pkg{i}", [128, 2048], I32)) for i in range(2)]
        nib = e(nc.sbuf_tensor("nib", [128, 2048], I32))
        wg16 = e(nc.sbuf_tensor("wg16", [128, 4096], BF16))
        wu16 = e(nc.sbuf_tensor("wu16", [128, 4096], BF16))
        hbuf = e(nc.sbuf_tensor("hbuf", [128, 1024], BF16))
        srep = e(nc.sbuf_tensor("srep", [128, 2048], BF16))
        zsl = e(nc.sbuf_tensor("zsl", [NIF, 512], BF16))
        hsum16 = e(nc.sbuf_tensor("hsum16", [NIF, TOK], BF16))
        sgb = e(nc.sbuf_tensor("sgb", [128, NT * 256], BF16))
        bp16 = e(nc.sbuf_tensor("bp16", [1, 256], BF16))
        fin32 = e(nc.sbuf_tensor("fin32", [128, 256], F32))
        fin32b = e(nc.sbuf_tensor("fin32b", [128, 256], F32))
        xfin = e(nc.sbuf_tensor("xfin", [128, 256], F32))
        zw = e(nc.sbuf_tensor("zw", [NIF, 86], I32))
        zsf = e(nc.sbuf_tensor("zsf", [NIF, 688], F32))
        zsb = e(nc.sbuf_tensor("zsb", [NIF, 688], BF16))
        ident = e(nc.sbuf_tensor("ident", [128, 128], BF16))
        iota_c = e(nc.sbuf_tensor("iota_c", [128, 1], F32))
        mt = e(nc.sbuf_tensor("mt", [128, 384], F32))
        ones_col = e(nc.sbuf_tensor("ones_col", [128, 1], BF16))
        ones_row = e(nc.sbuf_tensor("ones_row", [1, 128], BF16))
        epsc = e(nc.sbuf_tensor("epsc", [128, 1], F32))

        psf = [e(nc.psum_tensor(f"psf{i}", [128, 512], F32)) for i in range(6)]
        psb = [e(nc.psum_tensor(f"psb{i}", [128, 1024], BF16)) for i in range(2)]

        seq = e(nc.semaphore("seq"))
        block = e(nc.Block())

        for (dst, src) in ((ident, ident_d), (iota_c, iota_d), (mt, mt_d),
                           (ones_col, onec_d), (ones_row, oner_d)):
            P.add("sync", lambda s, dst=dst, src=src: s.dma_start(
                out=dst[:], in_=src), dma=True)
        P.add("sync", lambda s: s.dma_start(
            out=bsb[:], in_=b_attn.rearrange("(t p) -> p t", p=128),
            allow_slow_non_contiguous=True), dma=True)
        P.add("vector", lambda v: v.tensor_scalar(
            out=bq[:], in0=bsb[:, 0:32], scalar1=1.0 / 128.0, scalar2=None,
            op0=ALU.mult))
        P.add("vector", lambda v: v.memset(epsc[:], 1e-5))
        for i in range(32):
            for half in range(2):
                P.add("gpsimd", lambda g, i=i, half=half: g.dma_start(
                    out=sb(big, 0, 128, half * FD, FD),
                    in_=w_attn[i * 128:(i + 1) * 128,
                               FD + half * FD:FD + (half + 1) * FD]), dma=True)
                P.add("gpsimd", lambda g, i=i, half=half: g.dma_start(
                    out=w16kv[i * 128:(i + 1) * 128, half * FD:(half + 1) * FD],
                    in_=sb(big, 0, 128, half * FD, FD)), dma=True)
        for (a, b) in ((gsc, gs16), (usc, us16), (dsc, ds16)):
            P.add("gpsimd", lambda g, a=a, b=b: g.dma_start(out=b, in_=a), dma=True)

        for (qz, sc32, dst, ng, wd) in ((gzr, gsc, zs_g, 32, IFD),
                                        (uzr, usc, zs_u, 32, IFD),
                                        (dzr, dsc, zs_d, NIF, OFD)):
            nwtot = wd // 8
            nw = 86 if wd == IFD else 64
            nchunks = nwtot // nw
            assert nchunks * nw == nwtot
            for cch in range(nchunks):
                P.add("sync", lambda s, qz=qz, cch=cch, ng=ng, nw=nw: s.dma_start(
                    out=sb(zw, 0, ng, 0, nw),
                    in_=qz[0:ng, cch * nw:(cch + 1) * nw]), dma=True)
                P.add("sync", lambda s, sc32=sc32, cch=cch, ng=ng, nw=nw: s.dma_start(
                    out=sb(zsf, 0, ng, 0, nw * 8),
                    in_=sc32[0:ng, cch * nw * 8:(cch + 1) * nw * 8]), dma=True)

                def unp(v, ng=ng, nw=nw):
                    za = zn_ap = zsb.ap()
                    for j in range(8):
                        o = bass.AP(zwn.ap().tensor, zwn.ap().offset + j,
                                    [[zwn.ap().ap[0][0], ng], [8, nw]])
                        ins = v.tensor_scalar(out=o, in0=sb(zw, 0, ng, 0, nw),
                                              scalar1=4 * j, scalar2=15,
                                              op0=ALU.logical_shift_right,
                                              op1=ALU.bitwise_and)
                    return ins
                zwn = nib
                P.add("vector", unp)
                P.add("vector", lambda v, ng=ng, nw=nw: v.scalar_tensor_tensor(
                    out=sb(zsb, 0, ng, 0, nw * 8), in0=sb(nib, 0, ng, 0, nw * 8),
                    scalar=-1.0, in1=sb(zsf, 0, ng, 0, nw * 8),
                    op0=ALU.mult, op1=ALU.mult))
                P.add("sync", lambda s, dst=dst, cch=cch, ng=ng, nw=nw: s.dma_start(
                    out=dst[0:ng, cch * nw * 8:(cch + 1) * nw * 8],
                    in_=sb(zsb, 0, ng, 0, nw * 8)), dma=True)

        def ln_tile(s, tt):
            row0 = s * SW + tt * 128
            P.add("gpsimd", lambda sy, row0=row0: sy.dma_start(
                out=xtile[:], in_=xk[row0:row0 + 128, :]), dma=True)
            P.add("vector", lambda v: v.reduce_sum(stats[:, 0:1], xtile[:],
                                                   axis=AX.X))
            for ch in range(8):
                P.add("scalar", lambda sc, ch=ch: sc.activation(
                    out=stg[0][:, 0:512], in_=xtile[:, ch * 512:(ch + 1) * 512],
                    func=AF.Square))
                P.add("vector", lambda v, ch=ch: v.reduce_sum(
                    stats[:, 9 + ch:10 + ch], stg[0][:, 0:512], axis=AX.X))
            P.add("vector", lambda v: v.reduce_sum(stats[:, 1:2],
                                                   stats[:, 9:17], axis=AX.X))

            def fin_stats(v):
                v.tensor_scalar(out=stats[:, 2:3], in0=stats[:, 0:1],
                                scalar1=1.0 / FD, scalar2=None, op0=ALU.mult)
                return v.tensor_scalar(out=stats[:, 3:4], in0=stats[:, 1:2],
                                       scalar1=1.0 / FD, scalar2=None,
                                       op0=ALU.mult)
            P.add("vector", fin_stats)
            P.add("vector", lambda v: v.tensor_tensor(
                out=stats[:, 4:5], in0=stats[:, 2:3], in1=stats[:, 2:3],
                op=ALU.mult))
            P.add("vector", lambda v: v.tensor_sub(
                stats[:, 5:6], stats[:, 3:4], stats[:, 4:5]))

            P.add("scalar", lambda sc: sc.activation(
                out=stats[:, 6:7], in_=stats[:, 5:6], func=AF.Ln,
                bias=epsc[:, 0:1]))
            P.add("scalar", lambda sc: sc.activation(
                out=stats[:, 7:8], in_=stats[:, 6:7], func=AF.Exp, scale=-0.5))
            P.add("vector", lambda v: v.scalar_tensor_tensor(
                out=stats[:, 8:9], in0=stats[:, 2:3], scalar=-1.0,
                in1=stats[:, 7:8], op0=ALU.mult, op1=ALU.mult))
            P.add("vector", lambda v: v.tensor_scalar(
                out=wsl[:], in0=xtile[:], scalar1=stats[:, 7:8],
                scalar2=stats[:, 8:9], op0=ALU.mult, op1=ALU.add))
            if s == NSW - 1:
                def xsums(v, tt=tt):
                    for c in range(32):
                        ins = v.reduce_sum(xsum[:, tt * 32 + c:tt * 32 + c + 1],
                                           wsl[:, c * 128:(c + 1) * 128],
                                           axis=AX.X)
                    return ins
                P.add("vector", xsums)
            for g4 in range(8):
                b = g4 % 2

                def tr(t, g4=g4, b=b):
                    for k in range(4):
                        r = g4 * 4 + k
                        ins = t.transpose(psb[b][:, k * 128:(k + 1) * 128],
                                          wsl[:, r * 128:(r + 1) * 128],
                                          ident[:])
                    return ins
                P.add("tensor", tr)

                def ev(sc, tt=tt, g4=g4, b=b):
                    for k in range(4):
                        r = g4 * 4 + k
                        ins = sc.activation(
                            out=sb(big, 0, 128, r * SW + tt * 128, 128),
                            in_=psb[b][:, k * 128:(k + 1) * 128], func=AF.Copy)
                    return ins
                P.add("scalar", ev)

        def kv_sweep(s):
            for h in range(NH):
                lo = S - EH[h]
                if (s + 1) * SW <= lo:
                    continue
                c0 = max(lo, s * SW)
                chunks = [(c, min(c + 512, (s + 1) * SW))
                          for c in range(c0, (s + 1) * SW, 512)]
                for kvi in range(2):
                    wcol = kvi * FD + h * 128
                    P.add("sync", lambda sy, wcol=wcol: sy.dma_start(
                        out=wsl[:],
                        in_=mkap(w16kv, wcol,
                                 [[2 * FD, 128], [128 * 2 * FD, 32], [1, 128]])),
                        dma=True)
                    for (ca, cb) in chunks:
                        n = cb - ca

                        def mmz(t, ca=ca, n=n, s=s):
                            for f in range(32):
                                ins = t.matmul(
                                    psf[0][:, 0:n],
                                    wsl[:, f * 128:(f + 1) * 128],
                                    sb(big, 0, 128, f * SW + (ca - s * SW), n),
                                    start=(f == 0), stop=(f == 31))
                            return ins
                        P.add("tensor", mmz)
                        if kvi == 0:
                            P.add("vector", lambda v, n=n, h=h: v.tensor_scalar(
                                out=stg[0][:, 0:n], in0=psf[0][:, 0:n],
                                scalar1=bsb[:, 32 + h:33 + h], scalar2=None,
                                op0=ALU.add))
                            P.add("sync", lambda sy, ca=ca, n=n, h=h: sy.dma_start(
                                out=kst[h][:, ca - (S - EH[h]):
                                           ca - (S - EH[h]) + n],
                                in_=stg[0][:, 0:n]), dma=True)
                        else:
                            P.add("vector", lambda v, n=n, h=h: v.tensor_scalar(
                                out=stg[1][:, 0:n], in0=psf[0][:, 0:n],
                                scalar1=bsb[:, 64 + h:65 + h], scalar2=None,
                                op0=ALU.add))

                            def trv(t, n=n):
                                for k in range(n // 128):
                                    ins = t.transpose(
                                        psb[0][:, k * 128:(k + 1) * 128],
                                        stg[1][:, k * 128:(k + 1) * 128],
                                        ident[:])
                                return ins
                            P.add("tensor", trv)
                            P.add("vector", lambda v, n=n: v.tensor_copy(
                                stg[2][:, 0:n], psb[0][:, 0:n]))
                            P.add("sync", lambda sy, ca=ca, n=n, h=h: sy.dma_start(
                                out=vst[h][:, ca - (S - EH[h]):
                                           ca - (S - EH[h]) + n],
                                in_=stg[2][:, 0:n]), dma=True)

        for s in range(NSW):
            for tt in range(8):
                ln_tile(s, tt)
            kv_sweep(s)

        for tt in range(NT):
            P.add("vector", lambda v, tt=tt: v.tensor_copy(
                xsum16[:, 0:32], xsum[:, tt * 32:(tt + 1) * 32]))
            P.add("tensor", lambda t, tt=tt: t.transpose(
                sb(psb[0], 0, 32, 0, 128), xsum16[:, 0:32], ident[:]))
            P.add("vector", lambda v, tt=tt: v.tensor_copy(
                sb(xsumT, 0, 32, tt * 128, 128), sb(psb[0], 0, 32, 0, 128)))

        for qf in range(32):
            P.add("gpsimd", lambda g, qf=qf: g.dma_start(
                out=wsl[:],
                in_=mkap(w_attn, qf * 128,
                         [[3 * FD, 128], [128 * 3 * FD, 32], [1, 128]])),
                dma=True)
            for half in range(2):
                def qmm(t, half=half):
                    for f in range(32):
                        ins = t.matmul(psf[0][:, 0:512],
                                       wsl[:, f * 128:(f + 1) * 128],
                                       sb(big, 0, 128, f * SW + half * 512, 512),
                                       start=(f == 0), stop=(f == 31))
                    return ins
                P.add("tensor", qmm)
                P.add("vector", lambda v, qf=qf: v.tensor_scalar(
                    out=stg[0][:, 0:512], in0=psf[0][:, 0:512],
                    scalar1=1.0 / 128.0, scalar2=bq[:, qf:qf + 1],
                    op0=ALU.mult, op1=ALU.add))
                P.add("sync", lambda sy, qf=qf, half=half: sy.dma_start(
                    out=qst[qf * 128:(qf + 1) * 128, half * 512:half * 512 + 512],
                    in_=stg[0][:, 0:512]), dma=True)

        for ift in range(NIF):
            grp, sub = ift // 4, ift % 4
            nwg = min(64, (IFD // 8) - grp * 64)
            if sub == 0:
                for mi, qw in enumerate((gqw, uqw)):
                    P.add("sync", lambda sy, qw=qw, grp=grp, mi=mi, nwg=nwg:
                          sy.dma_start(
                              out=sb(pkg[mi], 0, 128, 0, 32 * nwg),
                              in_=mkap(qw, grp * 64,
                                       [[IFD // 8, 128], [128 * (IFD // 8), 32],
                                        [1, nwg]])), dma=True)
            P.add("sync", lambda sy, ift=ift: sy.dma_start(
                out=sb(zsl, 0, 32, 0, 128),
                in_=zs_g[:, ift * 128:(ift + 1) * 128]), dma=True)
            P.add("sync", lambda sy, ift=ift: sy.dma_start(
                out=sb(zsl, 0, 32, 128, 128),
                in_=zs_u[:, ift * 128:(ift + 1) * 128]), dma=True)
            for mi, (s16, w16) in enumerate(((gs16, wg16), (us16, wu16))):
                for rh in range(2):
                    src = bass.AP(s16.tensor, s16.offset + ift * 128,
                                  [[0, 128], [IFD, 16], [1, 128]])
                    src = bass.AP(s16.tensor,
                                  s16.offset + ift * 128 + rh * 16 * IFD,
                                  [[0, 128], [IFD, 16], [1, 128]])
                    P.add("sync", lambda sy, src=src: sy.dma_start(
                        out=srep[:, 0:2048], in_=src), dma=True)

                    def unp(v, mi=mi, sub=sub, nwg=nwg, rh=rh):
                        for j in range(8):
                            o = bass.AP(nib.ap().tensor, nib.ap().offset + j,
                                        [[nib.ap().ap[0][0], 128], [128, 16],
                                         [8, 16]])
                            ins = v.tensor_scalar(
                                out=o,
                                in0=bass.AP(pkg[mi].ap().tensor,
                                            pkg[mi].ap().offset + sub * 16
                                            + rh * 16 * nwg,
                                            [[pkg[mi].ap().ap[0][0], 128],
                                             [nwg, 16], [1, 16]]),
                                scalar1=4 * j, scalar2=15,
                                op0=ALU.logical_shift_right,
                                op1=ALU.bitwise_and)
                        return ins
                    P.add("vector", unp)
                    P.add("vector", lambda v, w16=w16, rh=rh: v.tensor_tensor(
                        out=sb(w16, 0, 128, rh * 2048, 2048),
                        in0=sb(nib, 0, 128, 0, 2048),
                        in1=srep[:, 0:2048], op=ALU.mult))

            def gumm(t):
                for mi, w16 in enumerate((wg16, wu16)):
                    for half in range(2):
                        bank = psf[mi * 2 + half]
                        t.matmul(bank[:, 0:512],
                                 sb(zsl, 0, 32, mi * 128, 128),
                                 sb(xsumT, 0, 32, half * 512, 512),
                                 start=True, stop=False)
                        for f in range(32):
                            ins = t.matmul(
                                bank[:, 0:512], w16[:, f * 128:(f + 1) * 128],
                                sb(big, 0, 128, f * SW + half * 512, 512),
                                start=False, stop=(f == 31))
                return ins
            P.add("tensor", gumm)

            def sil(sc):
                for half in range(2):
                    ins = sc.activation(out=stg[half][:, 0:512],
                                        in_=psf[half][:, 0:512], func=AF.Silu)
                return ins
            P.add("scalar", sil)

            def hmul(v):
                for half in range(2):
                    ins = v.scalar_tensor_tensor(
                        out=stg[2 + half][:, 0:512], in0=psf[2 + half][:, 0:512],
                        scalar=1.0, in1=stg[half][:, 0:512],
                        op0=ALU.mult, op1=ALU.mult)
                return ins
            P.add("vector", hmul)

            def hsm(t):
                t.matmul(sb(psf[4], 0, 1, 0, 512), ones_col[:],
                         stg[2][:, 0:512], start=True, stop=True)
                return t.matmul(sb(psf[5], 0, 1, 0, 512), ones_col[:],
                                stg[3][:, 0:512], start=True, stop=True)
            P.add("tensor", hsm)
            for half in range(2):
                P.add("vector", lambda v, half=half: v.tensor_copy(
                    rowf[0:1, 0:512], sb(psf[4 + half], 0, 1, 0, 512)))
                P.add("sync", lambda sy, ift=ift, half=half: sy.dma_start(
                    out=sb(hsum16, ift, 1, half * 512, 512),
                    in_=rowf[0:1, 0:512]), dma=True)
                P.add("sync", lambda sy, ift=ift, half=half: sy.dma_start(
                    out=hst[ift * 128:(ift + 1) * 128,
                            half * 512:half * 512 + 512],
                    in_=stg[2 + half][:, 0:512]), dma=True)

        for h in range(NH):
            E = EH[h]
            nkc = E // 128
            g = GH[h]
            sl = SLOPES[h]
            P.add("sync", lambda sy, h=h: sy.dma_start(
                out=qbuf[:], in_=qst[h * 128:(h + 1) * 128, :]), dma=True)
            P.add("sync", lambda sy, h=h: sy.dma_start(
                out=phsh[:], in_=phant[:, h * 64:(h + 1) * 64]), dma=True)

            nch = E // 512
            for half in range(2):
                P.add("scalar", lambda sc, half=half: sc.activation(
                    out=stg[0][:, 0:512],
                    in_=qbuf[:, half * 512:half * 512 + 512], func=AF.Square))
                P.add("tensor", lambda t: t.matmul(
                    sb(psf[5], 0, 1, 0, 512), ones_col[:], stg[0][:, 0:512],
                    start=True, stop=True))
                P.add("vector", lambda v, half=half: v.reduce_max(
                    sb(stats, 0, 1, 20 + half, 1), sb(psf[5], 0, 1, 0, 512),
                    axis=AX.X))
            P.add("vector", lambda v: v.reduce_max(
                nrm[0:1, 0:1], sb(stats, 0, 1, 20, 2), axis=AX.X))
            for ci, c5 in enumerate(range(0, E, 512)):
                n = min(512, E - c5)
                P.add("sync", lambda sy, h=h, c5=c5, n=n: sy.dma_start(
                    out=stg[2][:, 0:n], in_=kst[h][:, c5:c5 + n]), dma=True)
                P.add("scalar", lambda sc, n=n: sc.activation(
                    out=stg[0][:, 0:n], in_=stg[2][:, 0:n], func=AF.Square))
                P.add("tensor", lambda t, n=n: t.matmul(
                    sb(psf[5], 0, 1, 0, n), ones_col[:], stg[0][:, 0:n],
                    start=True, stop=True))
                P.add("vector", lambda v, ci=ci, n=n: v.reduce_max(
                    sb(stats, 0, 1, ci, 1), sb(psf[5], 0, 1, 0, n), axis=AX.X))
            P.add("vector", lambda v, nch=nch: v.reduce_max(
                nrm[0:1, 1:2], sb(stats, 0, 1, 0, nch), axis=AX.X))
            P.add("vector", lambda v: v.tensor_tensor(
                out=nrm[0:1, 3:4], in0=nrm[0:1, 0:1], in1=nrm[0:1, 1:2],
                op=ALU.mult))
            P.add("scalar", lambda sc: sc.activation(
                out=nrm[0:1, 4:5], in_=nrm[0:1, 3:4], func=AF.Sqrt))
            P.add("sync", lambda sy, h=h: sy.dma_start(
                out=mrow[0:1, h:h + 1], in_=nrm[0:1, 4:5]), dma=True)
            P.add("sync", lambda sy, h=h: sy.dma_start(
                out=m_rep[:],
                in_=bass.AP(mrow.tensor, mrow.offset + h, [[0, 128], [1, 1]])),
                dma=True)
            P.add("vector", lambda v, sl=sl: v.tensor_scalar(
                out=base_h[:], in0=iota_c[:], scalar1=sl,
                scalar2=m_rep[:, 0:1], op0=ALU.mult, op1=ALU.subtract))

            wins = [win_range(h, lt) for lt in range(NT)]
            out_started = [False, False]
            den_started = False
            bias_slot = 0
            pslot = 0
            for kc in range(nkc):
                elig = [lt for lt in range(NT)
                        if wins[lt][0] <= kc * 128 < wins[lt][1]]
                if not elig:
                    continue
                P.add("sync", lambda sy, h=h, kc=kc: sy.dma_start(
                    out=ktile[:], in_=kst[h][:, kc * 128:(kc + 1) * 128]),
                    dma=True)
                P.add("sync", lambda sy, h=h, kc=kc: sy.dma_start(
                    out=vtile[:], in_=vst[h][:, kc * 128:(kc + 1) * 128]),
                    dma=True)
                segs = []
                a0 = elig[0]
                while a0 <= elig[-1]:
                    b0 = min(a0 + 3, elig[-1])
                    segs.append((a0, b0))
                    a0 = b0 + 1
                for (sA, sB) in segs:
                    tq0 = sA * 128
                    span = (sB - sA + 1) * 128
                    pt = ptile[pslot]
                    pslot ^= 1
                    P.add("tensor", lambda t, tq0=tq0, span=span: t.matmul(
                        psf[4][:, 0:span], ktile[:], qbuf[:, tq0:tq0 + span],
                        start=True, stop=True))
                    for lt in range(sA, sB + 1):
                        Dk = E - TOK + 128 * lt - 128 * kc
                        if Dk < 127:
                            off = 128 + Dk
                            P.add("vector", lambda v, lt=lt, sA=sA, off=off:
                                  v.tensor_tensor(
                                      out=psf[4][:, (lt - sA) * 128:(lt - sA + 1) * 128],
                                      in0=psf[4][:, (lt - sA) * 128:(lt - sA + 1) * 128],
                                      in1=mt[:, off:off + 128], op=ALU.add))
                    b_lo = tq0 // g
                    b_hi = (tq0 + span + g - 1) // g
                    for bb in range(b_lo, b_hi):
                        c0 = max(tq0, bb * g)
                        c1 = min(tq0 + span, (bb + 1) * g)
                        imm = sl * (128 * kc + TOK - E - bb * g) + MARGIN
                        slot = bias_slot % 32
                        bias_slot += 1
                        P.add("vector", lambda v, slot=slot, imm=imm, kc=kc:
                              v.scalar_tensor_tensor(
                                  out=biasv[:, slot:slot + 1], in0=base_h[:],
                                  scalar=imm, in1=phsh[:, kc:kc + 1],
                                  op0=ALU.add, op1=ALU.add))
                        P.add("scalar", lambda sc, pt=pt, c0=c0, c1=c1,
                              tq0=tq0, slot=slot: sc.activation(
                                  out=pt[:, c0 - tq0:c1 - tq0],
                                  in_=psf[4][:, c0 - tq0:c1 - tq0],
                                  func=AF.Exp, bias=biasv[:, slot:slot + 1]))
                    for lt in range(sA, sB + 1):
                        bi2 = lt // 4
                        ob = psf[2 + bi2]
                        oslot = (lt % 4) * 128
                        st_o = not out_started[bi2]
                        out_started[bi2] = True
                        st_d = not den_started
                        den_started = True
                        last = (kc == wins[lt][1] // 128 - 1)

                        def pv(t, pt=pt, lt=lt, sA=sA, ob=ob, oslot=oslot,
                               st_o=st_o, st_d=st_d, last=last):
                            t.matmul(ob[:, oslot:oslot + 128],
                                     pt[:, (lt - sA) * 128:(lt - sA + 1) * 128],
                                     vtile[:], start=st_o, stop=last)
                            return t.matmul(psf[5][:, 256 + lt:257 + lt],
                                            pt[:, (lt - sA) * 128:(lt - sA + 1) * 128],
                                            ones_col[:], start=st_d, stop=last)
                        P.add("tensor", pv)

            P.add("vector", lambda v: v.reciprocal(recipv[:, 0:NT],
                                                   psf[5][:, 256:256 + NT]))

            def evo(v, h=h):
                for lt in range(NT):
                    ob = psf[2 + lt // 4]
                    oslot = (lt % 4) * 128
                    ins = v.tensor_scalar(
                        out=sb(big, 0, 128, lt * FD + h * 128, 128),
                        in0=ob[:, oslot:oslot + 128],
                        scalar1=recipv[:, lt:lt + 1], scalar2=None, op0=ALU.mult)
                return ins
            P.add("vector", evo)

        for tt in range(NT):
            for g8 in range(8):
                b = g8 % 2

                def tr(t, tt=tt, g8=g8, b=b):
                    for k in range(4):
                        r = g8 * 4 + k
                        ins = t.transpose(
                            psb[b][:, k * 128:(k + 1) * 128],
                            sb(big, 0, 128, tt * FD + r * 128, 128), ident[:])
                    return ins
                P.add("tensor", tr)

                def ev(sc, b=b):
                    for k in range(4):
                        ins = sc.activation(out=stg[k][:, 0:128],
                                            in_=psb[b][:, k * 128:(k + 1) * 128],
                                            func=AF.Copy)
                    return ins
                P.add("scalar", ev)
                for k in range(4):
                    r = g8 * 4 + k
                    P.add("sync", lambda sy, tt=tt, r=r, k=k: sy.dma_start(
                        out=atst[r * 128:(r + 1) * 128, tt * 128:(tt + 1) * 128],
                        in_=stg[k][:, 0:128]), dma=True)

        for opas in range(16):
            oc = opas * 256
            P.add("sync", lambda sy, oc=oc: sy.dma_start(
                out=sb(zsl, 0, NIF, 0, 256), in_=zs_d[:, oc:oc + 256]), dma=True)
            dn_started = [False] * 4
            for gi in range(NIF):
                ggrp, gsub = gi // 8, gi % 8
                if gsub == 0:
                    ng = min(8, NIF - ggrp * 8)
                    P.add("sync", lambda sy, ggrp=ggrp, oc=oc, ng=ng: sy.dma_start(
                        out=sb(pkg[0], 0, 128, 0, ng * 32),
                        in_=mkap(dqw, ggrp * 1024 * (OFD // 8) + oc // 8,
                                 [[OFD // 8, 128], [128 * (OFD // 8), ng],
                                  [1, 32]])), dma=True)
                    src = bass.AP(ds16.tensor, ds16.offset + ggrp * 8 * OFD + oc,
                                  [[0, 128], [OFD, ng], [1, 256]])
                    P.add("sync", lambda sy, src=src, ng=ng: sy.dma_start(
                        out=srep[:, 0:ng * 256], in_=src), dma=True)

                    def unp(v, ng=ng):
                        for j in range(8):
                            o = bass.AP(nib.ap().tensor, nib.ap().offset + j,
                                        [[nib.ap().ap[0][0], 128], [256, ng],
                                         [8, 32]])
                            ins = v.tensor_scalar(
                                out=o,
                                in0=bass.AP(pkg[0].ap().tensor,
                                            pkg[0].ap().offset,
                                            [[pkg[0].ap().ap[0][0], 128],
                                             [32, ng], [1, 32]]),
                                scalar1=4 * j, scalar2=15,
                                op0=ALU.logical_shift_right,
                                op1=ALU.bitwise_and)
                        return ins
                    P.add("vector", unp)
                    P.add("vector", lambda v, ng=ng: v.tensor_tensor(
                        out=sb(wg16, 0, 128, 0, ng * 256),
                        in0=sb(nib, 0, 128, 0, ng * 256),
                        in1=srep[:, 0:ng * 256], op=ALU.mult))
                P.add("sync", lambda sy, gi=gi: sy.dma_start(
                    out=hbuf[:], in_=hst[gi * 128:(gi + 1) * 128, :]), dma=True)

                def dmm(t, gsub=gsub, dn_started=dn_started):
                    for tt in range(NT):
                        bank = psf[tt // 2]
                        bslot = (tt % 2) * 256
                        st = not dn_started[tt // 2]
                        dn_started[tt // 2] = True
                        ins = t.matmul(bank[:, bslot:bslot + 256],
                                       hbuf[:, tt * 128:(tt + 1) * 128],
                                       sb(wg16, 0, 128, gsub * 256, 256),
                                       start=st, stop=False)
                    return ins
                P.add("tensor", dmm)

            def dz(t):
                for tt in range(NT):
                    bank = psf[tt // 2]
                    bslot = (tt % 2) * 256
                    ins = t.matmul(bank[:, bslot:bslot + 256],
                                   sb(hsum16, 0, NIF, tt * 128, 128),
                                   sb(zsl, 0, NIF, 0, 256),
                                   start=False, stop=True)
                return ins
            P.add("tensor", dz)

            def gsil(sc):
                for tt in range(NT):
                    bank = psf[tt // 2]
                    bslot = (tt % 2) * 256
                    ins = sc.activation(out=sgb[:, tt * 256:(tt + 1) * 256],
                                        in_=bank[:, bslot:bslot + 256],
                                        func=AF.Silu)
                return ins
            P.add("scalar", gsil)

            pj_started = [False] * 4
            for f in range(32):
                P.add("gpsimd", lambda gq, f=f, oc=oc: gq.dma_start(
                    out=wsl[:, 0:256],
                    in_=w_proj[f * 128:(f + 1) * 128, oc:oc + 256]), dma=True)
                P.add("sync", lambda sy, f=f: sy.dma_start(
                    out=wsl[:, 1024:2048],
                    in_=atst[f * 128:(f + 1) * 128, :]), dma=True)

                def pmm(t, pj_started=pj_started):
                    for tt in range(NT):
                        bank = psf[tt // 2]
                        bslot = (tt % 2) * 256
                        st = not pj_started[tt // 2]
                        pj_started[tt // 2] = True
                        ins = t.matmul(bank[:, bslot:bslot + 256],
                                       wsl[:, 1024 + tt * 128:1024 + (tt + 1) * 128],
                                       wsl[:, 0:256], start=st, stop=False)
                    return ins
                P.add("tensor", pmm)

            P.add("gpsimd", lambda gq, oc=oc: gq.dma_start(
                out=bp16[0:1, 0:256], in_=b_proj[oc:oc + 256]), dma=True)

            def pbias(t):
                for tt in range(NT):
                    bank = psf[tt // 2]
                    bslot = (tt % 2) * 256
                    ins = t.matmul(bank[:, bslot:bslot + 256], ones_row[:],
                                   bp16[0:1, 0:256], start=False, stop=True)
                return ins
            P.add("tensor", pbias)

            for tt in range(NT):
                bank = psf[tt // 2]
                bslot = (tt % 2) * 256
                P.add("sync", lambda sy, tt=tt, oc=oc: sy.dma_start(
                    out=xfin[:], in_=xk[S - TOK + tt * 128:S - TOK + (tt + 1) * 128,
                                        oc:oc + 256]), dma=True)
                P.add("vector", lambda v, bank=bank, bslot=bslot, tt=tt:
                      v.scalar_tensor_tensor(
                          out=fin32[:], in0=bank[:, bslot:bslot + 256],
                          scalar=1.0, in1=sgb[:, tt * 256:(tt + 1) * 256],
                          op0=ALU.mult, op1=ALU.mult))
                P.add("vector", lambda v: v.tensor_add(fin32b[:], fin32[:],
                                                       xfin[:]))
                P.add("sync", lambda sy, tt=tt, oc=oc: sy.dma_start(
                    out=out[tt * 128:(tt + 1) * 128, oc:oc + 256],
                    in_=fin32b[:]), dma=True)

        P.emit(block, seq)
    return nc


_NC_CACHE = {}


def kernel(L0_attn_hidden_states, L0_attn_w_attn, L0_attn_b_attn,
           L0_attn_w_proj, L0_attn_b_proj,
           L0_ffn_gate_qweight, L0_ffn_gate_scales, L0_ffn_gate_qzeros,
           L0_ffn_up_qweight, L0_ffn_up_scales, L0_ffn_up_qzeros,
           L0_ffn_down_qweight, L0_ffn_down_scales, L0_ffn_down_qzeros):
    x = np.asarray(L0_attn_hidden_states, dtype=np.float32).reshape(S, FD)
    w_attn = np.ascontiguousarray(np.asarray(L0_attn_w_attn, np.float32))
    b_attn = np.ascontiguousarray(np.asarray(L0_attn_b_attn, np.float32))
    w_proj = np.ascontiguousarray(np.asarray(L0_attn_w_proj, np.float32))
    b_proj = np.ascontiguousarray(np.asarray(L0_attn_b_proj, np.float32))
    gqw = np.ascontiguousarray(np.asarray(L0_ffn_gate_qweight, np.int32))
    gsc = np.ascontiguousarray(np.asarray(L0_ffn_gate_scales, np.float32))
    gzr = np.ascontiguousarray(np.asarray(L0_ffn_gate_qzeros, np.int32))
    uqw = np.ascontiguousarray(np.asarray(L0_ffn_up_qweight, np.int32))
    usc = np.ascontiguousarray(np.asarray(L0_ffn_up_scales, np.float32))
    uzr = np.ascontiguousarray(np.asarray(L0_ffn_up_qzeros, np.int32))
    dqw = np.ascontiguousarray(np.asarray(L0_ffn_down_qweight, np.int32))
    dsc = np.ascontiguousarray(np.asarray(L0_ffn_down_scales, np.float32))
    dzr = np.ascontiguousarray(np.asarray(L0_ffn_down_qzeros, np.int32))

    if "nc" not in _NC_CACHE:
        _NC_CACHE["nc"] = build_nc()
    nc = _NC_CACHE["nc"]

    in_maps = []
    ii = np.arange(128)
    for c in range(8):
        B1 = TOK * (c + 1)
        pad = S - B1
        xkc = np.zeros((S, FD), dtype=np.float32)
        xkc[pad:, :] = x[0:B1, :]
        ph = np.zeros((128, NH * 64), dtype=np.float32)
        for h in range(NH):
            nkc = EH[h] // 128
            for kc in range(nkc):
                glob = B1 - EH[h] + 128 * kc + ii
                ph[glob < 0, h * 64 + kc] = -1e30
        in_maps.append(dict(
            xk=xkc, w_attn=w_attn, b_attn=b_attn, w_proj=w_proj, b_proj=b_proj,
            gqw=gqw, gsc=gsc, gzr=gzr, uqw=uqw, usc=usc, uzr=uzr,
            dqw=dqw, dsc=dsc, dzr=dzr, phant=ph))

    res = run_bass_kernel_spmd(nc, in_maps, core_ids=list(range(8)))
    o = np.concatenate([np.asarray(r["out"], np.float32) for r in res.results],
                       axis=0)
    return o.reshape(1, S, FD)
